# version 16
# baseline (speedup 1.0000x reference)
import numpy as np
import ml_dtypes
from contextlib import ExitStack

import concourse.bass as bass
import concourse.mybir as mybir
from concourse.bass_utils import run_bass_kernel_spmd

F32 = mybir.dt.float32
BF16 = mybir.dt.bfloat16
AF = mybir.ActivationFunctionType
ALU = mybir.AluOpType
bfdt = ml_dtypes.bfloat16

NCORE = 8
B, S, D, E = 16, 4096, 1024, 8
BPC = B // NCORE
TOK = BPC * S
TA = 512
NTA = TOK // TA
TPB = S // TA
TF = 512
NTF = TOK // TF
TPBF = S // TF
DC = D // 128
EPS = 1e-5


class P:
    def __init__(self, nc, stack):
        self.nc = nc
        self.sems = {}
        self.cnt = {}
        self.seen = {}
        names = ["t", "v", "s",
                 "qx0", "qx1", "qx2", "qx3", "qw0", "qw1", "qb",
                 "qf0", "qf1", "qy0", "qy1", "qe0", "qe1", "qe2", "qe3", "qc",
                 "qs40", "qs41",
                 "gx10", "gx11", "gyp0", "gyp1", "gsel", "gout0", "gout1"]
        for name in names:
            self.sems[name] = stack.enter_context(nc.semaphore("sem_" + name))
            self.cnt[name] = 0
        self.eng = {"q": nc.sync, "g": nc.gpsimd, "t": nc.tensor,
                    "v": nc.vector, "s": nc.scalar}

    def allcnt(self):
        return dict(self.cnt)

    def inc(self, instr, prod, n=1):
        instr.then_inc(self.sems[prod], n)
        self.cnt[prod] += n
        return self.cnt[prod]

    def wait(self, consumer, prod, val):
        if not val or val <= 0:
            return
        key = (consumer, prod)
        if self.seen.get(key, 0) >= val:
            return
        self.eng[consumer].wait_ge(self.sems[prod], val)
        self.seen[key] = val


def build_program():
    nc = bass.Bass("TRN2", target_bir_lowering=False)

    x_fm = nc.dram_tensor("x_fm", [BPC, D, S], F32, kind="ExternalInput").ap()
    wzt = nc.dram_tensor("wzt", [D, D], F32, kind="ExternalInput").ap()
    wht = nc.dram_tensor("wht", [D, D], F32, kind="ExternalInput").ap()
    wot = nc.dram_tensor("wot", [D, D], F32, kind="ExternalInput").ap()
    wexp = nc.dram_tensor("wexp", [E, D, D], F32, kind="ExternalInput").ap()
    bz_d = nc.dram_tensor("bz_d", [D], F32, kind="ExternalInput").ap()
    bh_d = nc.dram_tensor("bh_d", [D], F32, kind="ExternalInput").ap()
    bo_d = nc.dram_tensor("bo_d", [D], F32, kind="ExternalInput").ap()
    rw_d = nc.dram_tensor("rw_d", [D, E], F32, kind="ExternalInput").ap()
    rb_d = nc.dram_tensor("rb_d", [1, E], F32, kind="ExternalInput").ap()
    eb_d = nc.dram_tensor("eb_d", [E, D], F32, kind="ExternalInput").ap()
    esel8_d = nc.dram_tensor("esel8_d", [E, E * 128], BF16, kind="ExternalInput").ap()
    esel4_d = nc.dram_tensor("esel4_d", [4, 4 * 128], BF16, kind="ExternalInput").ap()
    out_fm = nc.dram_tensor("out_fm", [BPC, D, S], F32, kind="ExternalOutput").ap()
    x1s = nc.dram_tensor("x1s", [BPC, DC, 128, S], BF16, kind="Internal").ap()
    yps = nc.dram_tensor("yps", [BPC, DC, 128, S], BF16, kind="Internal").ap()
    sels = nc.dram_tensor("sels", [E, TOK], BF16, kind="Internal").ap()

    stack = ExitStack()
    with stack:
        p = P(nc, stack)
        rec = {}

        ones_row = stack.enter_context(nc.sbuf_tensor("ones_row", [1, 128], F32))
        ones_col_bf = stack.enter_context(nc.sbuf_tensor("ones_col_bf", [128, 1], BF16))
        ones8_bf = stack.enter_context(nc.sbuf_tensor("ones8_bf", [8, 1], BF16))
        ones_tf_bf = stack.enter_context(nc.sbuf_tensor("ones_tf_bf", [1, TF], BF16))
        eps_sb = stack.enter_context(nc.sbuf_tensor("eps_sb", [1, 1], F32))

        nc.vector.memset(ones_row[:], 1.0)
        nc.vector.memset(eps_sb[:], float(EPS))
        nc.vector.memset(ones_col_bf[:], 1.0)
        nc.vector.memset(ones8_bf[:], 1.0)
        v_ones = p.inc(nc.vector.memset(ones_tf_bf[:], 1.0), "v")

        stA = ExitStack()
        with stA:
            sbA = lambda n, shape, d=F32: stA.enter_context(nc.sbuf_tensor(n, shape, d))
            psA = lambda n, shape: stA.enter_context(nc.psum_tensor(n, shape, F32))

            wz_sb = sbA("wz_sb", [128, DC * D], BF16)
            wh_sb = sbA("wh_sb", [128, DC * D], BF16)
            wo_sb = sbA("wo_sb", [128, DC * D], BF16)
            bz_sb = sbA("bz_sb", [128, 8], F32)
            nbz_sb = sbA("nbz_sb", [128, 8], F32)
            bh_sb = sbA("bh_sb", [128, 8], F32)
            bh05_sb = sbA("bh05_sb", [128, 8], F32)
            bo_sb = sbA("bo_sb", [128, 8], F32)
            xst = sbA("xst", [128, 4 * TA], F32)
            wstA = sbA("wstA", [128, 2 * D], F32)
            x_bf = [sbA(f"x_bf{i}", [128, DC * TA], BF16) for i in range(3)]
            c_bf = [sbA(f"c_bf{i}", [128, DC * TA], BF16) for i in range(2)]
            v_bf = [sbA(f"v_bf{i}", [128, DC * TA], BF16) for i in range(2)]
            z_ch = [sbA(f"z_ch{i}", [128, TA], BF16) for i in range(2)]
            a1_ch = [sbA(f"a1_ch{i}", [128, TA], BF16) for i in range(2)]
            a2_ch = [sbA(f"a2_ch{i}", [128, TA], BF16) for i in range(2)]
            gt_ch = [sbA(f"gt_ch{i}", [128, TA], BF16) for i in range(2)]
            h_bf = [sbA(f"h_bf{i}", [128, DC * TA], BF16) for i in range(2)]
            carry = sbA("carry", [128, 8], F32)
            s1_bf = [sbA(f"s1_bf{i}", [128, DC * TA], BF16) for i in range(2)]
            sq_bf = sbA("sq_bf", [128, DC * TA], BF16)
            x1t = [sbA(f"x1t{i}", [128, DC * TA], BF16) for i in range(2)]
            xtmp_ch = [sbA(f"xtmp_ch{i}", [128, TA], F32) for i in range(2)]
            rows_m = sbA("rows_m", [1, TA], F32)
            rows_a = sbA("rows_a", [1, TA], F32)
            rows_b = sbA("rows_b", [1, TA], F32)
            rows_t = sbA("rows_t", [1, TA], F32)
            rows_sd = sbA("rows_sd", [1, TA], F32)

            ps_k = [psA(f"ps_k{i}", [128, TA]) for i in range(2)]
            ps_th = [psA(f"ps_th{i}", [128, TA]) for i in range(2)]
            ps_att = [psA(f"ps_att{i}", [128, TA]) for i in range(2)]
            ps_st1 = psA("ps_st1", [128, TA])
            ps_st2 = psA("ps_st2", [128, TA])

            for wi, (src, dst) in enumerate([(wzt, wz_sb), (wht, wh_sb), (wot, wo_sb)]):
                for kc in range(DC):
                    g = wi * DC + kc
                    par = g % 2
                    p.wait("g", "v", rec.get(("wcast", g - 2), 0))
                    qv = p.inc(nc.gpsimd.dma_start(
                        wstA[:, par * D:(par + 1) * D],
                        src[kc * 128:(kc + 1) * 128, :]), f"qw{par}", 16)
                    p.wait("v", f"qw{par}", qv)
                    rec[("wcast", g)] = p.inc(nc.vector.tensor_copy(
                        dst[:, kc * D:(kc + 1) * D], wstA[:, par * D:(par + 1) * D]), "v")
            for src, dst in [(bz_d, bz_sb), (bh_d, bh_sb), (bo_d, bo_sb)]:
                for cix in range(8):
                    p.inc(nc.sync.dma_start(dst[:, cix:cix + 1],
                                            src[cix * 128:(cix + 1) * 128]), "qb", 16)
            p.wait("v", "qb", p.cnt["qb"])
            nc.vector.tensor_scalar_mul(nbz_sb[:], bz_sb[:], -1.0)
            v_bias = p.inc(nc.vector.tensor_scalar_add(bh05_sb[:], bh_sb[:], 0.5), "v")
            v_wdone = rec[("wcast", 3 * DC - 1)]

            def loadA(t):
                b, soff = t // TPB, (t % TPB) * TA
                for dc in range(DC):
                    g = t * DC + dc
                    par = g % 4
                    p.wait("q", "v", rec.get(("xcast", g - 4), 0))
                    qv = p.inc(nc.sync.dma_start(
                        xst[:, par * TA:(par + 1) * TA],
                        x_fm[b, dc * 128:(dc + 1) * 128, soff:soff + TA]), f"qx{par}", 16)
                    p.wait("v", f"qx{par}", qv)
                    p.wait("v", "v", rec.get(("s1", t - 3, DC - 1), 0))
                    rec[("xcast", g)] = p.inc(nc.vector.tensor_copy(
                        x_bf[t % 3][:, dc * TA:(dc + 1) * TA],
                        xst[:, par * TA:(par + 1) * TA]), "v")
                rec[("xcast_t", t)] = rec[("xcast", t * DC + DC - 1)]

            def kth(t):
                first = (t % TPB) == 0
                p.wait("s", "v", v_bias)
                for hc in range(DC):
                    par = hc % 2
                    p.wait("t", "v", rec[("xcast_t", t)])
                    p.wait("t", "v", v_wdone)
                    p.wait("t", "s", rec.get(("cz", t, hc - 2),
                                             rec.get(("cz", t - 1, hc + 6), 0)))
                    for kc in range(DC):
                        mm = nc.tensor.matmul(
                            ps_k[par][:],
                            wz_sb[:, kc * D + hc * 128: kc * D + (hc + 1) * 128],
                            x_bf[t % 3][:, kc * TA:(kc + 1) * TA],
                            start=(kc == 0), stop=(kc == DC - 1))
                    rec[("kmm", t, hc)] = p.inc(mm, "t")
                    p.wait("t", "s", rec.get(("a12", t, hc - 2),
                                             rec.get(("a12", t - 1, hc + 6), 0)))
                    for kc in range(DC):
                        mm = nc.tensor.matmul(
                            ps_th[par][:],
                            wh_sb[:, kc * D + hc * 128: kc * D + (hc + 1) * 128],
                            x_bf[t % 3][:, kc * TA:(kc + 1) * TA],
                            start=(kc == 0), stop=(kc == DC - 1))
                    rec[("thmm", t, hc)] = p.inc(mm, "t")
                    p.wait("s", "t", rec[("kmm", t, hc)])
                    p.wait("s", "v", rec.get(("gv", t, hc - 2),
                                             rec.get(("gv", t - 1, hc + 6), 0)))
                    nc.scalar.activation(c_bf[t % 2][:, hc * TA:(hc + 1) * TA],
                                         ps_k[par][:], AF.Sigmoid,
                                         bias=nbz_sb[:, hc:hc + 1], scale=-1.0)
                    rec[("cz", t, hc)] = p.inc(
                        nc.scalar.activation(z_ch[par][:], ps_k[par][:], AF.Sigmoid,
                                             bias=bz_sb[:, hc:hc + 1], scale=1.0), "s")
                    p.wait("s", "t", rec[("thmm", t, hc)])
                    nc.scalar.activation(a1_ch[par][:], ps_th[par][:], AF.Sigmoid,
                                         bias=bh_sb[:, hc:hc + 1], scale=1.0)
                    rec[("a12", t, hc)] = p.inc(
                        nc.scalar.activation(a2_ch[par][:], ps_th[par][:], AF.Copy), "s")
                    p.wait("v", "s", rec[("a12", t, hc)])
                    nc.vector.scalar_tensor_tensor(gt_ch[par][:], a2_ch[par][:],
                                                   bh05_sb[:, hc:hc + 1], a1_ch[par][:],
                                                   ALU.add, ALU.max)
                    rec[("gv", t, hc)] = p.inc(
                        nc.vector.tensor_mul(v_bf[t % 2][:, hc * TA:(hc + 1) * TA],
                                             z_ch[par][:], gt_ch[par][:]), "v")

            def kthScan(t):
                first = (t % TPB) == 0
                for hc in range(DC):
                    sl = slice(hc * TA, (hc + 1) * TA)
                    if first:
                        init = 0.0
                    else:
                        init = carry[:, hc:hc + 1]
                        p.wait("v", "s", rec[("carry", t - 1, hc)])
                    rec[("scan", t, hc)] = p.inc(nc.vector.tensor_tensor_scan(
                        h_bf[t % 2][:, sl], c_bf[t % 2][:, sl], v_bf[t % 2][:, sl],
                        init, ALU.mult, ALU.add), "v")
                rec[("scan_t", t)] = rec[("scan", t, DC - 1)]

            def carryStage(t):
                for hc in range(DC):
                    p.wait("s", "v", rec[("scan", t, hc)])
                    rec[("carry", t, hc)] = p.inc(
                        nc.scalar.activation(
                            carry[:, hc:hc + 1],
                            h_bf[t % 2][:, hc * TA + TA - 1: hc * TA + TA],
                            AF.Copy), "s")

            def attstage(t):
                for dc in range(DC):
                    par = dc % 2
                    p.wait("t", "v", rec[("scan_t", t)])
                    p.wait("t", "v", rec.get(("s1", t, dc - 2),
                                             rec.get(("s1", t - 1, dc + 6), 0)))
                    for hc in range(DC):
                        mm = nc.tensor.matmul(
                            ps_att[par][:],
                            wo_sb[:, hc * D + dc * 128: hc * D + (dc + 1) * 128],
                            h_bf[t % 2][:, hc * TA:(hc + 1) * TA],
                            start=(hc == 0), stop=(hc == DC - 1))
                    rec[("attmm", t, dc)] = p.inc(mm, "t")
                    sl = slice(dc * TA, (dc + 1) * TA)
                    p.wait("v", "t", rec[("attmm", t, dc)])
                    p.wait("v", "t", rec.get(("stats", t - 2), 0))
                    rec[("s1", t, dc)] = p.inc(
                        nc.vector.scalar_tensor_tensor(
                            s1_bf[t % 2][:, sl], ps_att[par][:], bo_sb[:, dc:dc + 1],
                            x_bf[t % 3][:, sl], ALU.add, ALU.add), "v")
                    p.wait("s", "v", rec[("s1", t, dc)])
                    p.wait("s", "t", rec.get(("stats", t - 1), 0))
                    rec[("sq", t, dc)] = p.inc(
                        nc.scalar.activation(sq_bf[:, sl], s1_bf[t % 2][:, sl],
                                             AF.Square), "s")

            def statsA(t):
                p.wait("t", "v", rec[("s1", t, DC - 1)])
                p.wait("t", "s", rec[("sq", t, DC - 1)])
                p.wait("t", "v", rec.get(("x1", t - 1, DC - 1), 0))
                for dc in range(DC):
                    sl = slice(dc * TA, (dc + 1) * TA)
                    nc.tensor.matmul(ps_st1[0:1, :], ones_col_bf[:, 0:1],
                                     s1_bf[t % 2][:, sl],
                                     start=(dc == 0), stop=(dc == DC - 1))
                for dc in range(DC):
                    sl = slice(dc * TA, (dc + 1) * TA)
                    mm = nc.tensor.matmul(ps_st2[0:1, :], ones_col_bf[:, 0:1],
                                          sq_bf[:, sl],
                                          start=(dc == 0), stop=(dc == DC - 1))
                rec[("stats", t)] = p.inc(mm, "t")

            def rowsA(t):
                p.wait("v", "t", rec[("stats", t)])
                nc.vector.tensor_scalar_mul(rows_m[:], ps_st1[0:1, :], -1.0 / D)
                nc.vector.tensor_mul(rows_t[:], rows_m[:], rows_m[:])
                rec[("veps", t)] = p.inc(
                    nc.vector.scalar_tensor_tensor(rows_t[:], ps_st2[0:1, :],
                                                   1.0 / D, rows_t[:],
                                                   ALU.mult, ALU.subtract), "v")
                p.wait("s", "v", rec[("veps", t)])
                rec[("sd", t)] = p.inc(
                    nc.scalar.activation(rows_sd[:], rows_t[:], AF.Sqrt,
                                         bias=eps_sb[0:1, 0:1], scale=1.0), "s")
                p.wait("v", "s", rec[("sd", t)])
                nc.vector.reciprocal(rows_a[:], rows_sd[:])
                rec[("rows", t)] = p.inc(
                    nc.vector.tensor_mul(rows_b[:], rows_m[:], rows_a[:]), "v")

            def bcastA(t):
                p.wait("t", "v", rec[("rows", t)])
                nc.tensor.matmul(ps_st1[:, :], ones_row[0:1, :], rows_a[:],
                                 start=True, stop=True)
                rec[("bcast", t)] = p.inc(
                    nc.tensor.matmul(ps_st2[:, :], ones_row[0:1, :], rows_b[:],
                                     start=True, stop=True), "t")

            def x1apply(t):
                for dc in range(DC):
                    sl = slice(dc * TA, (dc + 1) * TA)
                    par = dc % 2
                    p.wait("v", "t", rec[("bcast", t)])
                    p.wait("v", f"gx1{t % 2}", rec.get(("x1st_t", t - 2), 0))
                    nc.vector.tensor_tensor(xtmp_ch[par][:], s1_bf[t % 2][:, sl],
                                            ps_st1[:, :], ALU.mult)
                    rec[("x1", t, dc)] = p.inc(
                        nc.vector.tensor_tensor(x1t[t % 2][:, sl], xtmp_ch[par][:],
                                                ps_st2[:, :], ALU.add), "v")

            def x1store(t):
                b, soff = t // TPB, (t % TPB) * TA
                for dc in range(DC):
                    sl = slice(dc * TA, (dc + 1) * TA)
                    p.wait("g", "v", rec[("x1", t, dc)])
                    rec[("x1st", t, dc)] = p.inc(
                        nc.gpsimd.dma_start(x1s[b, dc, :, soff:soff + TA],
                                            x1t[t % 2][:, sl]), f"gx1{t % 2}", 16)
                rec[("x1st_t", t)] = rec[("x1st", t, DC - 1)]

            loadA(0)
            for s in range(NTA + 2):
                if s < NTA:
                    kth(s)
                if 0 <= s - 1 < NTA:
                    carryStage(s - 1)
                if s - 2 >= 0:
                    statsA(s - 2)
                    rowsA(s - 2)
                if s + 1 < NTA:
                    loadA(s + 1)
                if 0 <= s - 1 < NTA:
                    attstage(s - 1)
                if s - 2 >= 0:
                    bcastA(s - 2)
                    x1apply(s - 2)
                    x1store(s - 2)
                if s < NTA:
                    kthScan(s)

            finA = p.allcnt()

        def barrier(fin):
            for cons in ("q", "g", "t", "v", "s"):
                for prod, val in fin.items():
                    ceng = {"q": "sync", "g": "gpsimd"}.get(cons)
                    p.wait(cons, prod, val)

        stF = ExitStack()
        with stF:
            sbF = lambda n, shape, d=F32: stF.enter_context(nc.sbuf_tensor(n, shape, d))
            wexp_sb = sbF("wexp_sb", [128, 4 * DC * D], BF16)
            wstF = sbF("wstF", [128, 4 * D], F32)
            rw_f32 = sbF("rw_f32", [128, 64], F32)
            rw_bf = sbF("rw_bf", [128, 64], BF16)
            rb_f32 = sbF("rb_f32", [1, 8], F32)
            rb_bf = sbF("rb_bf", [1, 8], BF16)
            eb_f32 = sbF("eb_f32", [4, D], F32)
            eb_bf = sbF("eb_bf", [4, D], BF16)
            esel_bf = sbF("esel_bf", [8, 8 * 128], BF16)
            x1_bf = [sbF(f"fx1_{i}", [128, DC * TF], BF16) for i in range(2)]
            xe_sb = [sbF(f"xe_{e}", [128, DC * TF], BF16) for e in range(4)]
            bsel = [sbF(f"bsel{i}", [128, 4 * TF], BF16) for i in range(2)]
            yp_t = [sbF(f"yp_t{i}", [128, DC * TF], BF16) for i in range(2)]

            ps_y = [stF.enter_context(nc.psum_tensor(f"ps_y{i}", [128, TF], F32))
                    for i in range(4)]
            ps_r1 = stF.enter_context(nc.psum_tensor("ps_r1", [128, TF], F32))
            ps_r2 = stF.enter_context(nc.psum_tensor("ps_r2", [128, TF], F32))
            ps_r3 = stF.enter_context(nc.psum_tensor("ps_r3", [128, TF], F32))

            barrier(finA)

            def load_experts(e_base, prewait=0, els=(0, 1, 2, 3)):
                if prewait:
                    p.wait("q", "t", prewait)
                    p.wait("g", "t", prewait)
                    p.wait("v", "t", prewait)
                for el in els:
                    for kc in range(DC):
                        g = el * DC + kc
                        par = g % 4
                        qpar = g % 2
                        qeng = nc.sync if qpar == 0 else nc.gpsimd
                        p.wait("q" if qpar == 0 else "g", "v",
                               rec.get(("ecast", e_base, g - 4), 0))
                        qv = p.inc(qeng.dma_start(
                            wstF[:, par * D:(par + 1) * D],
                            wexp[e_base + el, kc * 128:(kc + 1) * 128, :]), f"qe{par}", 16)
                        p.wait("v", f"qe{par}", qv)
                        rec[("ecast", e_base, g)] = p.inc(nc.vector.tensor_copy(
                            wexp_sb[:, (el * DC + kc) * D:(el * DC + kc + 1) * D],
                            wstF[:, par * D:(par + 1) * D]), "v")
                    rec[("ewx", e_base, el)] = rec[("ecast", e_base, el * DC + DC - 1)]
                    if el == 3:
                        rec[("ew", e_base)] = rec[("ecast", e_base, 4 * DC - 1)]

            def load_consts_f(pass_ix):
                if pass_ix == 0:
                    for kc in range(DC):
                        p.inc(nc.sync.dma_start(rw_f32[:, kc * 8:(kc + 1) * 8],
                                                rw_d[kc * 128:(kc + 1) * 128, :]), "qc", 16)
                    p.inc(nc.sync.dma_start(rb_f32[:], rb_d), "qc", 16)
                    p.inc(nc.sync.dma_start(esel_bf[:], esel8_d), "qc", 16)
                    qv = p.inc(nc.sync.dma_start(eb_f32[:], eb_d[0:4, :]), "qc", 16)
                    p.wait("v", "qc", qv)
                    nc.vector.tensor_copy(rw_bf[:], rw_f32[:])
                    nc.vector.tensor_copy(rb_bf[:], rb_f32[:])
                else:
                    p.inc(nc.sync.dma_start(esel_bf[0:4, 0:4 * 128], esel4_d), "qc", 16)
                    qv = p.inc(nc.sync.dma_start(eb_f32[:], eb_d[4:8, :]), "qc", 16)
                    p.wait("v", "qc", qv)
                rec[("fconsts", pass_ix)] = p.inc(
                    nc.vector.tensor_copy(eb_bf[:], eb_f32[:]), "v")

            def load_x1(t, pass_ix):
                b, soff = t // TPBF, (t % TPBF) * TF
                par = t % 2
                for dc in range(DC):
                    p.wait("q", "v", rec.get(("xe", pass_ix, t - 2, 3, DC - 1), 0))
                    if pass_ix == 1:
                        p.wait("q", "v", rec.get(("s2", t - 2, DC - 1), 0))
                    p.inc(nc.sync.dma_start(
                        x1_bf[par][:, dc * TF:(dc + 1) * TF],
                        x1s[b, dc, :, soff:soff + TF]), f"qf{par}", 16)
                rec[("x1ld", t)] = p.cnt[f"qf{par}"]

            def load_yp(t):
                b, soff = t // TPBF, (t % TPBF) * TF
                par = t % 2
                for dc in range(DC):
                    p.wait("q", "v", rec.get(("s2", t - 2, DC - 1), 0))
                    p.inc(nc.sync.dma_start(
                        yp_t[par][:, dc * TF:(dc + 1) * TF],
                        yps[b, dc, :, soff:soff + TF]), f"qy{par}", 16)
                rec[("ypld", t)] = p.cnt[f"qy{par}"]

            ps_r4 = stF.enter_context(nc.psum_tensor("ps_r4", [128, TF], F32))

            stF0 = ExitStack()
            with stF0:
                sb0 = lambda n, shape, d=F32: stF0.enter_context(
                    nc.sbuf_tensor(n, shape, d))
                l_fm = [sb0(f"l_fm{i}", [8, TF], F32) for i in range(2)]
                e_fm = [sb0(f"e_fm{i}", [8, TF], BF16) for i in range(2)]
                oh_fm = [sb0(f"oh_fm{i}", [8, TF], F32) for i in range(2)]
                sel_fm = [sb0(f"sel_fm{i}", [8, TF], BF16) for i in range(2)]
                sh1 = sb0("sh1", [8, TF], F32)
                m4 = sb0("m4", [4, TF], F32)
                t2_ = sb0("t2_", [4, TF], F32)
                m2 = sb0("m2", [2, TF], F32)
                t3_ = sb0("t3_", [2, TF], F32)
                m_row = [sb0(f"m_row{i}", [1, TF], F32) for i in range(2)]
                num_row = [sb0(f"num_row{i}", [1, TF], F32) for i in range(2)]
                gate_row = [sb0(f"gate_row{i}", [1, TF], F32) for i in range(2)]
                rden_row = sb0("rden_row", [1, TF], F32)

                MASK4 = [(q + 4) % 32 for q in range(32)]
                MASK2 = [(q + 2) % 32 for q in range(32)]
                MASK1 = [(q + 1) % 32 for q in range(32)]

                def routerA(t):
                    par = t % 2
                    p.wait("t", f"qf{t % 2}", rec[("x1ld", t)])
                    p.wait("t", "v", rec[("fconsts", 0)])
                    p.wait("t", "s", rec.get(("lfm", t - 1), 0))
                    for kc in range(DC):
                        nc.tensor.matmul(ps_r1[0:8, :], rw_bf[:, kc * 8:(kc + 1) * 8],
                                         x1_bf[t % 2][:, kc * TF:(kc + 1) * TF],
                                         start=(kc == 0), stop=False)
                    rec[("lmm", t)] = p.inc(
                        nc.tensor.matmul(ps_r1[0:8, :], rb_bf[0:1, :],
                                         ones_tf_bf[0:1, :], start=False, stop=True), "t")
                    p.wait("s", "t", rec[("lmm", t)])
                    nc.scalar.activation(l_fm[par][:], ps_r1[0:8, :], AF.Copy)
                    rec[("lfm", t)] = p.inc(
                        nc.scalar.activation(e_fm[par][:], ps_r1[0:8, :], AF.Exp), "s")
                    p.wait("v", "s", rec[("lfm", t)])
                    nc.vector.stream_shuffle(sh1[:], l_fm[par][:], MASK4)
                    nc.vector.tensor_max(m4[:], sh1[0:4, :], l_fm[par][0:4, :])
                    nc.vector.stream_shuffle(t2_[:], m4[:], MASK2)
                    nc.vector.tensor_max(m2[:], m4[0:2, :], t2_[0:2, :])
                    nc.vector.stream_shuffle(t3_[:], m2[:], MASK1)
                    rec[("mrow", t)] = p.inc(
                        nc.vector.tensor_max(m_row[par][:], m2[0:1, :], t3_[0:1, :]), "v")

                def routerB(t):
                    par = t % 2
                    p.wait("t", "v", rec[("mrow", t)])
                    rec[("mbc", t)] = p.inc(
                        nc.tensor.matmul(ps_r2[0:8, :], ones_row[0:1, 0:8],
                                         m_row[par][:], start=True, stop=True), "t")
                    rec[("den", t)] = p.inc(
                        nc.tensor.matmul(ps_r3[0:1, :], ones8_bf[:, 0:1], e_fm[par][:],
                                         start=True, stop=True), "t")
                    p.wait("v", "t", rec[("mbc", t)])
                    rec[("oh", t)] = p.inc(
                        nc.vector.tensor_tensor(oh_fm[par][:], l_fm[par][:],
                                                ps_r2[0:8, :], ALU.is_ge), "v")
                    p.wait("s", "v", rec[("mrow", t)])
                    rec[("num", t)] = p.inc(
                        nc.scalar.activation(num_row[par][:], m_row[par][:], AF.Exp), "s")
                    p.wait("v", "t", rec[("den", t)])
                    p.wait("v", "s", rec[("num", t)])
                    nc.vector.reciprocal(rden_row[:], ps_r3[0:1, :])
                    rec[("gate", t)] = p.inc(
                        nc.vector.tensor_mul(gate_row[par][:], num_row[par][:],
                                             rden_row[:]), "v")

                def routerC(t):
                    par = t % 2
                    p.wait("t", "v", rec[("oh", t)])
                    p.wait("t", "v", rec[("gate", t)])
                    rec[("gbc", t)] = p.inc(
                        nc.tensor.matmul(ps_r2[0:8, :], ones_row[0:1, 0:8],
                                         gate_row[par][:], start=True, stop=True), "t")
                    p.wait("v", "t", rec[("gbc", t)])
                    rec[("sel", t)] = p.inc(
                        nc.vector.tensor_tensor(sel_fm[par][:], oh_fm[par][:],
                                                ps_r2[0:8, :], ALU.mult), "v")
                    p.wait("g", "v", rec[("sel", t)])
                    rec[("selst", t)] = p.inc(
                        nc.gpsimd.dma_start(sels[:, t * TF:(t + 1) * TF], sel_fm[par][:]),
                        "gsel", 16)

                def routerD(t):
                    par = t % 2
                    for e in range(4):
                        bank = ps_r4 if e % 2 == 0 else ps_r3
                        p.wait("t", "v", rec[("sel", t)])
                        if e >= 2:
                            p.wait("t", "s", rec.get(("bselev", t, e - 2), 0))
                        rec[("bselmm", t, e)] = p.inc(
                            nc.tensor.matmul(bank[:, :],
                                             esel_bf[:, e * 128:(e + 1) * 128],
                                             sel_fm[par][:], start=True, stop=True), "t")
                        p.wait("s", "t", rec[("bselmm", t, e)])
                        p.wait("s", "v", rec.get(("xe", 0, t - 2, 3, DC - 1), 0))
                        rec[("bselev", t, e)] = p.inc(
                            nc.scalar.activation(bsel[par][:, e * TF:(e + 1) * TF],
                                                 bank[:, :], AF.Copy), "s")

                def xestage(t, pass_ix):
                    par = t % 2
                    for e in range(4):
                        for kc in range(DC):
                            p.wait("v", f"qf{t % 2}", rec[("x1ld", t)])
                            if pass_ix == 0:
                                p.wait("v", "s", rec[("bselev", t, e)])
                            else:
                                p.wait("v", "s", rec[("bselev1", t, e)])
                            p.wait("v", "t",
                                   rec.get(("emm", pass_ix, t - 1, 1, e, DC - 1), 0))
                            rec[("xe", pass_ix, t, e, kc)] = p.inc(
                                nc.vector.tensor_mul(
                                    xe_sb[e][:, kc * TF:(kc + 1) * TF],
                                    x1_bf[par][:, kc * TF:(kc + 1) * TF],
                                    bsel[par][:, e * TF:(e + 1) * TF]), "v")

                def expertsHalf(t, pass_ix, e_base, half, evac=True):
                    par = t % 2
                    for e in range(4):
                        p.wait("t", "v", rec[("ewx", e_base, e)])
                        for kc in range(DC):
                            p.wait("t", "v", rec[("xe", pass_ix, t, e, kc)])
                            if half == 0 and e == 0 and kc == 0 and pass_ix == 1:
                                p.wait("t", "v", rec.get(("s2", t - 1, DC - 1), 0))
                            if half == 1 and e == 0 and kc == 0:
                                p.wait("t", "v", rec.get(("yev", pass_ix, t, 0, 3), 0))
                            for dcl in range(4):
                                dc = half * 4 + dcl
                                mm = nc.tensor.matmul(
                                    ps_y[dcl][:],
                                    wexp_sb[:, (e * DC + kc) * D + dc * 128:
                                            (e * DC + kc) * D + (dc + 1) * 128],
                                    xe_sb[e][:, kc * TF:(kc + 1) * TF],
                                    start=(e == 0 and kc == 0), stop=False)
                            rec[("emm", pass_ix, t, half, e, kc)] = p.inc(mm, "t")
                    p.wait("t", "v", rec[("fconsts", pass_ix)])
                    if pass_ix == 0:
                        p.wait("t", "v", rec[("sel", t)])
                        selsrc = sel_fm[par][0:4, :]
                    else:
                        p.wait("t", f"qs4{t % 2}", rec[("sel4ld", t)])
                        selsrc = sel4_t[par][0:4, :]
                    for dcl in range(4):
                        dc = half * 4 + dcl
                        mm = nc.tensor.matmul(
                            ps_y[dcl][:], eb_bf[0:4, dc * 128:(dc + 1) * 128],
                            selsrc, start=False, stop=True)
                        rec[("bias", pass_ix, t, half, dcl)] = p.inc(mm, "t")
                    if not evac:
                        return
                    for dcl in range(4):
                        dc = half * 4 + dcl
                        sl = slice(dc * TF, (dc + 1) * TF)
                        p.wait("v", "t", rec[("bias", pass_ix, t, half, dcl)])
                        if pass_ix == 0:
                            p.wait("v", f"gyp{t % 2}", rec.get(("ypst_t", t - 2), 0))
                            rec[("yev", pass_ix, t, half, dcl)] = p.inc(
                                nc.vector.tensor_copy(yp_t[par][:, sl],
                                                      ps_y[dcl][:]), "v")
                        else:
                            p.wait("v", f"qy{t % 2}", rec[("ypld", t)])
                            p.wait("v", "t", rec.get(("stats2", t - 2), 0))
                            nc.vector.tensor_tensor(tmpf_ch[dc % 2][:], ps_y[dcl][:],
                                                    yp_t[par][:, sl], ALU.add)
                            rec[("s2", t, dc)] = p.inc(
                                nc.vector.tensor_tensor(
                                    s2_bf[par][:, sl], tmpf_ch[dc % 2][:],
                                    x1_bf[par][:, sl], ALU.add), "v")
                            rec[("yev", pass_ix, t, half, dcl)] = rec[("s2", t, dc)]

                def sqstage(t, half):
                    for dcl in range(4):
                        dc = half * 4 + dcl
                        sl = slice(dc * TF, (dc + 1) * TF)
                        p.wait("s", "v", rec[("s2", t, dc)])
                        p.wait("s", "t", rec.get(("stats2", t - 1), 0))
                        rec[("sq2", t, dc)] = p.inc(
                            nc.scalar.activation(sq2_bf[:, sl], s2_bf[t % 2][:, sl],
                                                 AF.Square), "s")

                def ypstore(t):
                    b, soff = t // TPBF, (t % TPBF) * TF
                    for dc in range(DC):
                        sl = slice(dc * TF, (dc + 1) * TF)
                        half, dcl = dc // 4, dc % 4
                        p.wait("g", "v", rec[("yev", 0, t, half, dcl)])
                        rec[("ypst", t, dc)] = p.inc(
                            nc.gpsimd.dma_start(yps[b, dc, :, soff:soff + TF],
                                                yp_t[t % 2][:, sl]), f"gyp{t % 2}", 16)
                    rec[("ypst_t", t)] = rec[("ypst", t, DC - 1)]

                load_consts_f(0)
                load_x1(0, 0)
                load_x1(1, 0)
                load_experts(0, els=(0,))
                routerA(0)
                routerB(0)
                routerC(0)
                routerD(0)
                load_experts(0, els=(1, 2, 3))
                xestage(0, 0)
                for s in range(NTF):
                    if s + 1 < NTF:
                        routerA(s + 1)
                    expertsHalf(s, 0, 0, 0)
                    if s + 1 < NTF:
                        routerB(s + 1)
                    expertsHalf(s, 0, 0, 1)
                    if s + 1 < NTF:
                        routerC(s + 1)
                        routerD(s + 1)
                    ypstore(s)
                    if s + 2 < NTF:
                        load_x1(s + 2, 0)
                    if s + 1 < NTF:
                        xestage(s + 1, 0)

                fin0 = p.allcnt()

            stF1 = ExitStack()
            with stF1:
                sb1 = lambda n, shape, d=F32: stF1.enter_context(
                    nc.sbuf_tensor(n, shape, d))
                sel4_t = [sb1(f"sel4_{i}", [4, TF], BF16) for i in range(2)]
                s2_bf = [sb1(f"s2_bf{i}", [128, DC * TF], BF16) for i in range(2)]
                sq2_bf = sb1("sq2_bf", [128, DC * TF], BF16)
                tmpf_ch = [sb1(f"tmpf_{i}", [128, TF], F32) for i in range(2)]
                x2_ch = [sb1(f"x2_{i}", [128, TF], F32) for i in range(2)]
                rows2_m = sb1("r2m", [1, TF], F32)
                rows2_a = sb1("r2a", [1, TF], F32)
                rows2_b = sb1("r2b", [1, TF], F32)
                rows2_t = sb1("r2t", [1, TF], F32)
                rows2_sd = sb1("r2sd", [1, TF], F32)

                barrier(fin0)
                load_consts_f(1)
                prew1 = rec[("emm", 0, NTF - 1, 1, 3, DC - 1)]

                def s2half1(t):
                    par = t % 2
                    for dcl in range(4):
                        dc = 4 + dcl
                        sl = slice(dc * TF, (dc + 1) * TF)
                        p.wait("v", "t", rec[("bias", 1, t, 1, dcl)])
                        p.wait("v", f"qy{t % 2}", rec[("ypld", t)])
                        p.wait("v", "t", rec.get(("stats2", t - 1), 0))
                        nc.vector.tensor_tensor(tmpf_ch[dc % 2][:], ps_y[dcl][:],
                                                yp_t[par][:, sl], ALU.add)
                        rec[("s2", t, dc)] = p.inc(
                            nc.vector.tensor_tensor(
                                s2_bf[par][:, sl], tmpf_ch[dc % 2][:],
                                x1_bf[par][:, sl], ALU.add), "v")
                        rec[("yev", 1, t, 1, dcl)] = rec[("s2", t, dc)]

                def load_sel4(t):
                    par = t % 2
                    p.wait("q", "t", rec.get(("bias", 1, t - 2, 1, 3), 0))
                    p.inc(nc.sync.dma_start(sel4_t[par][:],
                                            sels[4:8, t * TF:(t + 1) * TF]), f"qs4{par}", 16)
                    rec[("sel4ld", t)] = p.cnt[f"qs4{par}"]

                def bsel1(t):
                    par = t % 2
                    for e in range(4):
                        bank = ps_r4 if e % 2 == 0 else ps_r3
                        p.wait("t", f"qs4{t % 2}", rec[("sel4ld", t)])
                        p.wait("t", "v", rec[("fconsts", 1)])
                        p.wait("t", "s", rec.get(("bselev1", t - 1, 3), 0))
                        if e >= 2:
                            p.wait("t", "s", rec.get(("bselev1", t, e - 2), 0))
                        rec[("bselmm1", t, e)] = p.inc(
                            nc.tensor.matmul(bank[:, :],
                                             esel_bf[0:4, e * 128:(e + 1) * 128],
                                             sel4_t[par][:], start=True, stop=True), "t")
                        p.wait("s", "t", rec[("bselmm1", t, e)])
                        p.wait("s", "v", rec.get(("xe", 1, t - 2, 3, DC - 1), 0))
                        rec[("bselev1", t, e)] = p.inc(
                            nc.scalar.activation(bsel[par][:, e * TF:(e + 1) * TF],
                                                 bank[:, :], AF.Copy), "s")

                def stats2(t):
                    p.wait("t", "v", rec[("s2", t, DC - 1)])
                    p.wait("t", "s", rec[("sq2", t, DC - 1)])
                    p.wait("t", "v", rec.get(("x2", t - 1, DC - 1), 0))
                    for dc in range(DC):
                        sl = slice(dc * TF, (dc + 1) * TF)
                        nc.tensor.matmul(ps_r1[0:1, :], ones_col_bf[:, 0:1],
                                         s2_bf[t % 2][:, sl], start=(dc == 0),
                                         stop=(dc == DC - 1))
                    for dc in range(DC):
                        sl = slice(dc * TF, (dc + 1) * TF)
                        mm = nc.tensor.matmul(ps_r2[0:1, :], ones_col_bf[:, 0:1],
                                              sq2_bf[:, sl], start=(dc == 0),
                                              stop=(dc == DC - 1))
                    rec[("stats2", t)] = p.inc(mm, "t")

                def rows2(t):
                    p.wait("v", "t", rec[("stats2", t)])
                    nc.vector.tensor_scalar_mul(rows2_m[:], ps_r1[0:1, :], -1.0 / D)
                    nc.vector.tensor_mul(rows2_t[:], rows2_m[:], rows2_m[:])
                    rec[("veps2", t)] = p.inc(
                        nc.vector.scalar_tensor_tensor(rows2_t[:], ps_r2[0:1, :],
                                                       1.0 / D, rows2_t[:],
                                                       ALU.mult, ALU.subtract), "v")
                    p.wait("s", "v", rec[("veps2", t)])
                    rec[("sd2", t)] = p.inc(
                        nc.scalar.activation(rows2_sd[:], rows2_t[:], AF.Sqrt,
                                             bias=eps_sb[0:1, 0:1], scale=1.0), "s")
                    p.wait("v", "s", rec[("sd2", t)])
                    nc.vector.reciprocal(rows2_a[:], rows2_sd[:])
                    rec[("rows2", t)] = p.inc(
                        nc.vector.tensor_mul(rows2_b[:], rows2_m[:], rows2_a[:]), "v")

                def bcast2(t):
                    p.wait("t", "v", rec[("rows2", t)])
                    nc.tensor.matmul(ps_r1[:, :], ones_row[0:1, :], rows2_a[:],
                                     start=True, stop=True)
                    rec[("bcast2", t)] = p.inc(
                        nc.tensor.matmul(ps_r2[:, :], ones_row[0:1, :], rows2_b[:],
                                         start=True, stop=True), "t")

                def x2apply(t):
                    b, soff = t // TPBF, (t % TPBF) * TF
                    for dc in range(DC):
                        sl = slice(dc * TF, (dc + 1) * TF)
                        par = dc % 2
                        p.wait("v", "t", rec[("bcast2", t)])
                        p.wait("v", f"gout{dc % 2}", rec.get(("outst", t, dc - 2),
                                                 rec.get(("outst", t - 1, dc + 6), 0)))
                        nc.vector.tensor_tensor(tmpf_ch[par][:], s2_bf[t % 2][:, sl],
                                                ps_r1[:, :], ALU.mult)
                        rec[("x2", t, dc)] = p.inc(
                            nc.vector.tensor_tensor(x2_ch[par][:], tmpf_ch[par][:],
                                                    ps_r2[:, :], ALU.add), "v")
                        p.wait("g", "v", rec[("x2", t, dc)])
                        rec[("outst", t, dc)] = p.inc(
                            nc.gpsimd.dma_start(
                                out_fm[b, dc * 128:(dc + 1) * 128, soff:soff + TF],
                                x2_ch[par][:]), f"gout{dc % 2}", 16)

                load_x1(0, 1)
                load_yp(0)
                load_sel4(0)
                bsel1(0)
                xestage(0, 1)
                load_x1(1, 1)
                load_yp(1)
                load_sel4(1)
                load_experts(4, prewait=prew1)
                for s in range(NTF):
                    expertsHalf(s, 1, 4, 0)
                    if s - 1 >= 0:
                        stats2(s - 1)
                        rows2(s - 1)
                    if s + 1 < NTF:
                        bsel1(s + 1)
                    sqstage(s, 0)
                    expertsHalf(s, 1, 4, 1, evac=False)
                    if s - 1 >= 0:
                        bcast2(s - 1)
                    if s + 1 < NTF:
                        xestage(s + 1, 1)
                    s2half1(s)
                    sqstage(s, 1)
                    if s - 1 >= 0:
                        x2apply(s - 1)
                    if s + 2 < NTF:
                        load_x1(s + 2, 1)
                        load_yp(s + 2)
                        load_sel4(s + 2)
                stats2(NTF - 1)
                rows2(NTF - 1)
                bcast2(NTF - 1)
                x2apply(NTF - 1)

                nc.gpsimd.wait_ge(p.sems["gout0"], p.cnt["gout0"])
                nc.gpsimd.wait_ge(p.sems["gout1"], p.cnt["gout1"])
    return nc


_NC_CACHE = {}


def kernel(L0_attn_x, L0_attn_Wz, L0_attn_bz, L0_attn_Wh, L0_attn_bh, L0_attn_Wo,
           L0_attn_bo, L0_ffn_router_w, L0_ffn_router_b, L0_ffn_expert_w,
           L0_ffn_expert_b):
    x = np.asarray(L0_attn_x, dtype=np.float32)
    wz = np.asarray(L0_attn_Wz, dtype=np.float32)
    wh = np.asarray(L0_attn_Wh, dtype=np.float32)
    wo = np.asarray(L0_attn_Wo, dtype=np.float32)
    bz = np.ascontiguousarray(np.asarray(L0_attn_bz, dtype=np.float32))
    bh = np.ascontiguousarray(np.asarray(L0_attn_bh, dtype=np.float32))
    bo = np.ascontiguousarray(np.asarray(L0_attn_bo, dtype=np.float32))
    rw = np.ascontiguousarray(np.asarray(L0_ffn_router_w, dtype=np.float32))
    rb = np.ascontiguousarray(
        np.asarray(L0_ffn_router_b, dtype=np.float32)).reshape(1, E)
    ew = np.ascontiguousarray(np.asarray(L0_ffn_expert_w, dtype=np.float32))
    eb = np.ascontiguousarray(np.asarray(L0_ffn_expert_b, dtype=np.float32))

    wzt = np.ascontiguousarray(wz.T)
    wht = np.ascontiguousarray(wh.T)
    wot = np.ascontiguousarray(wo.T)

    esel8 = np.zeros((E, E * 128), bfdt)
    for e in range(E):
        esel8[e, e * 128:(e + 1) * 128] = bfdt(1.0)
    esel4 = np.zeros((4, 4 * 128), bfdt)
    for e in range(4):
        esel4[e, e * 128:(e + 1) * 128] = bfdt(1.0)

    if "nc" not in _NC_CACHE:
        _NC_CACHE["nc"] = build_program()
    nc = _NC_CACHE["nc"]

    in_maps = []
    for i in range(NCORE):
        xs = np.ascontiguousarray(x[i * BPC:(i + 1) * BPC].transpose(0, 2, 1))
        in_maps.append({
            "x_fm": xs, "wzt": wzt, "wht": wht, "wot": wot, "wexp": ew,
            "bz_d": bz, "bh_d": bh, "bo_d": bo, "rw_d": rw, "rb_d": rb,
            "eb_d": eb, "esel8_d": esel8, "esel4_d": esel4,
        })
    res = run_bass_kernel_spmd(nc, in_maps, core_ids=list(range(NCORE)))
    outs = [np.asarray(r["out_fm"]).transpose(0, 2, 1) for r in res.results]
    return np.ascontiguousarray(np.concatenate(outs, axis=0)).astype(np.float32)


# version 17
# speedup vs baseline: 1.1061x; 1.1061x over previous
import numpy as np
import ml_dtypes
from contextlib import ExitStack

import concourse.bass as bass
import concourse.mybir as mybir
from concourse.bass_utils import run_bass_kernel_spmd

F32 = mybir.dt.float32
BF16 = mybir.dt.bfloat16
AF = mybir.ActivationFunctionType
ALU = mybir.AluOpType
bfdt = ml_dtypes.bfloat16

NCORE = 8
B, S, D, E = 16, 4096, 1024, 8
BPC = B // NCORE
TOK = BPC * S
TA = 512
NTA = TOK // TA
TPB = S // TA
TF = 512
NTF = TOK // TF
TPBF = S // TF
DC = D // 128
EPS = 1e-5


class P:
    def __init__(self, nc, stack):
        self.nc = nc
        self.sems = {}
        self.cnt = {}
        self.seen = {}
        names = ["t", "v", "s",
                 "qx0", "qx1", "qx2", "qx3", "qw0", "qw1", "qb",
                 "qf0", "qf1", "qy0", "qy1", "qe0", "qe1", "qe2", "qe3", "qc",
                 "qs40", "qs41",
                 "gx10", "gx11", "gyp0", "gyp1", "gsel", "gout0", "gout1", "gc"]
        for name in names:
            self.sems[name] = stack.enter_context(nc.semaphore("sem_" + name))
            self.cnt[name] = 0
        self.eng = {"q": nc.sync, "g": nc.gpsimd, "t": nc.tensor,
                    "v": nc.vector, "s": nc.scalar}

    def allcnt(self):
        return dict(self.cnt)

    def inc(self, instr, prod, n=1):
        instr.then_inc(self.sems[prod], n)
        self.cnt[prod] += n
        return self.cnt[prod]

    def wait(self, consumer, prod, val):
        if not val or val <= 0:
            return
        key = (consumer, prod)
        if self.seen.get(key, 0) >= val:
            return
        self.eng[consumer].wait_ge(self.sems[prod], val)
        self.seen[key] = val


def build_program():
    nc = bass.Bass("TRN2", target_bir_lowering=False)

    x_fm = nc.dram_tensor("x_fm", [BPC, D, S], F32, kind="ExternalInput").ap()
    wzt = nc.dram_tensor("wzt", [D, D], F32, kind="ExternalInput").ap()
    wht = nc.dram_tensor("wht", [D, D], F32, kind="ExternalInput").ap()
    wot = nc.dram_tensor("wot", [D, D], F32, kind="ExternalInput").ap()
    wexp = nc.dram_tensor("wexp", [E, D, D], F32, kind="ExternalInput").ap()
    bz_d = nc.dram_tensor("bz_d", [D], F32, kind="ExternalInput").ap()
    bh_d = nc.dram_tensor("bh_d", [D], F32, kind="ExternalInput").ap()
    bo_d = nc.dram_tensor("bo_d", [D], F32, kind="ExternalInput").ap()
    rw_d = nc.dram_tensor("rw_d", [D, E], F32, kind="ExternalInput").ap()
    rb_d = nc.dram_tensor("rb_d", [1, E], F32, kind="ExternalInput").ap()
    eb_d = nc.dram_tensor("eb_d", [E, D], F32, kind="ExternalInput").ap()
    esel8_d = nc.dram_tensor("esel8_d", [E, E * 128], BF16, kind="ExternalInput").ap()
    esel4_d = nc.dram_tensor("esel4_d", [4, 4 * 128], BF16, kind="ExternalInput").ap()
    out_fm = nc.dram_tensor("out_fm", [BPC, D, S], F32, kind="ExternalOutput").ap()
    x1s = nc.dram_tensor("x1s", [BPC, DC, 128, S], BF16, kind="Internal").ap()
    yps = nc.dram_tensor("yps", [BPC, DC, 128, S], BF16, kind="Internal").ap()
    sels = nc.dram_tensor("sels", [E, TOK], BF16, kind="Internal").ap()

    stack = ExitStack()
    with stack:
        p = P(nc, stack)
        rec = {}

        ones_row = stack.enter_context(nc.sbuf_tensor("ones_row", [1, 128], F32))
        ones_col_bf = stack.enter_context(nc.sbuf_tensor("ones_col_bf", [128, 1], BF16))
        ones8_bf = stack.enter_context(nc.sbuf_tensor("ones8_bf", [8, 1], BF16))
        ones_tf_bf = stack.enter_context(nc.sbuf_tensor("ones_tf_bf", [1, TF], BF16))
        eps_sb = stack.enter_context(nc.sbuf_tensor("eps_sb", [1, 1], F32))

        nc.vector.memset(ones_row[:], 1.0)
        nc.vector.memset(eps_sb[:], float(EPS))
        nc.vector.memset(ones_col_bf[:], 1.0)
        nc.vector.memset(ones8_bf[:], 1.0)
        v_ones = p.inc(nc.vector.memset(ones_tf_bf[:], 1.0), "v")

        stA = ExitStack()
        with stA:
            sbA = lambda n, shape, d=F32: stA.enter_context(nc.sbuf_tensor(n, shape, d))
            psA = lambda n, shape: stA.enter_context(nc.psum_tensor(n, shape, F32))

            wz_sb = sbA("wz_sb", [128, DC * D], BF16)
            wh_sb = sbA("wh_sb", [128, DC * D], BF16)
            wo_sb = sbA("wo_sb", [128, DC * D], BF16)
            bz_sb = sbA("bz_sb", [128, 8], F32)
            nbz_sb = sbA("nbz_sb", [128, 8], F32)
            bh_sb = sbA("bh_sb", [128, 8], F32)
            bh05_sb = sbA("bh05_sb", [128, 8], F32)
            bo_sb = sbA("bo_sb", [128, 8], F32)
            xst = sbA("xst", [128, 4 * TA], F32)
            wstA = sbA("wstA", [128, 2 * D], F32)
            x_bf = [sbA(f"x_bf{i}", [128, DC * TA], BF16) for i in range(3)]
            c_bf = [sbA(f"c_bf{i}", [128, DC * TA], BF16) for i in range(2)]
            v_bf = [sbA(f"v_bf{i}", [128, DC * TA], BF16) for i in range(2)]
            z_ch = [sbA(f"z_ch{i}", [128, TA], BF16) for i in range(2)]
            a1_ch = [sbA(f"a1_ch{i}", [128, TA], BF16) for i in range(2)]
            a2_ch = [sbA(f"a2_ch{i}", [128, TA], BF16) for i in range(2)]
            gt_ch = [sbA(f"gt_ch{i}", [128, TA], BF16) for i in range(2)]
            h_bf = [sbA(f"h_bf{i}", [128, DC * TA], BF16) for i in range(2)]
            carry = sbA("carry", [128, 8], F32)
            s1_bf = [sbA(f"s1_bf{i}", [128, DC * TA], BF16) for i in range(2)]
            s1tmp = sbA("s1tmp", [128, DC * TA], BF16)
            sq_bf = sbA("sq_bf", [128, DC * TA], BF16)
            x1t = [sbA(f"x1t{i}", [128, DC * TA], BF16) for i in range(2)]
            xtmp_ch = [sbA(f"xtmp_ch{i}", [128, TA], F32) for i in range(2)]
            rows_m = sbA("rows_m", [1, TA], F32)
            rows_a = sbA("rows_a", [1, TA], F32)
            rows_b = sbA("rows_b", [1, TA], F32)
            rows_t = sbA("rows_t", [1, TA], F32)
            rows_sd = sbA("rows_sd", [1, TA], F32)

            ps_k = [psA(f"ps_k{i}", [128, TA]) for i in range(2)]
            ps_th = [psA(f"ps_th{i}", [128, TA]) for i in range(2)]
            ps_att = [psA(f"ps_att{i}", [128, TA]) for i in range(2)]
            ps_st1 = psA("ps_st1", [128, TA])
            ps_st2 = psA("ps_st2", [128, TA])

            for wi, (src, dst) in enumerate([(wzt, wz_sb), (wht, wh_sb), (wot, wo_sb)]):
                for kc in range(DC):
                    g = wi * DC + kc
                    par = g % 2
                    p.wait("g", "v", rec.get(("wcast", g - 2), 0))
                    qv = p.inc(nc.gpsimd.dma_start(
                        wstA[:, par * D:(par + 1) * D],
                        src[kc * 128:(kc + 1) * 128, :]), f"qw{par}", 16)
                    p.wait("v", f"qw{par}", qv)
                    rec[("wcast", g)] = p.inc(nc.vector.tensor_copy(
                        dst[:, kc * D:(kc + 1) * D], wstA[:, par * D:(par + 1) * D]), "v")
            for src, dst in [(bz_d, bz_sb), (bh_d, bh_sb), (bo_d, bo_sb)]:
                for cix in range(8):
                    p.inc(nc.sync.dma_start(dst[:, cix:cix + 1],
                                            src[cix * 128:(cix + 1) * 128]), "qb", 16)
            p.wait("v", "qb", p.cnt["qb"])
            nc.vector.tensor_scalar_mul(nbz_sb[:], bz_sb[:], -1.0)
            v_bias = p.inc(nc.vector.tensor_scalar_add(bh05_sb[:], bh_sb[:], 0.5), "v")
            v_wdone = rec[("wcast", 3 * DC - 1)]

            def loadA(t):
                b, soff = t // TPB, (t % TPB) * TA
                for dc in range(DC):
                    g = t * DC + dc
                    par = g % 4
                    p.wait("q", "v", rec.get(("xcast", g - 4), 0))
                    qv = p.inc(nc.sync.dma_start(
                        xst[:, par * TA:(par + 1) * TA],
                        x_fm[b, dc * 128:(dc + 1) * 128, soff:soff + TA]), f"qx{par}", 16)
                    p.wait("v", f"qx{par}", qv)
                    p.wait("v", "v", rec.get(("s1", t - 3, DC - 1), 0))
                    rec[("xcast", g)] = p.inc(nc.vector.tensor_copy(
                        x_bf[t % 3][:, dc * TA:(dc + 1) * TA],
                        xst[:, par * TA:(par + 1) * TA]), "v")
                rec[("xcast_t", t)] = rec[("xcast", t * DC + DC - 1)]

            def kth(t):
                first = (t % TPB) == 0
                p.wait("s", "v", v_bias)
                for hc in range(DC):
                    par = hc % 2
                    p.wait("t", "v", rec[("xcast_t", t)])
                    p.wait("t", "v", v_wdone)
                    p.wait("t", "s", rec.get(("cz", t, hc - 2),
                                             rec.get(("cz", t - 1, hc + 6), 0)))
                    for kc in range(DC):
                        mm = nc.tensor.matmul(
                            ps_k[par][:],
                            wz_sb[:, kc * D + hc * 128: kc * D + (hc + 1) * 128],
                            x_bf[t % 3][:, kc * TA:(kc + 1) * TA],
                            start=(kc == 0), stop=(kc == DC - 1))
                    rec[("kmm", t, hc)] = p.inc(mm, "t")
                    p.wait("t", "s", rec.get(("a12", t, hc - 2),
                                             rec.get(("a12", t - 1, hc + 6), 0)))
                    for kc in range(DC):
                        mm = nc.tensor.matmul(
                            ps_th[par][:],
                            wh_sb[:, kc * D + hc * 128: kc * D + (hc + 1) * 128],
                            x_bf[t % 3][:, kc * TA:(kc + 1) * TA],
                            start=(kc == 0), stop=(kc == DC - 1))
                    rec[("thmm", t, hc)] = p.inc(mm, "t")
                    p.wait("s", "t", rec[("kmm", t, hc)])
                    p.wait("s", "v", rec.get(("gv", t, hc - 2),
                                             rec.get(("gv", t - 1, hc + 6), 0)))
                    nc.scalar.activation(c_bf[t % 2][:, hc * TA:(hc + 1) * TA],
                                         ps_k[par][:], AF.Sigmoid,
                                         bias=nbz_sb[:, hc:hc + 1], scale=-1.0)
                    rec[("cz", t, hc)] = p.inc(
                        nc.scalar.activation(z_ch[par][:], ps_k[par][:], AF.Sigmoid,
                                             bias=bz_sb[:, hc:hc + 1], scale=1.0), "s")
                    p.wait("s", "t", rec[("thmm", t, hc)])
                    nc.scalar.activation(a1_ch[par][:], ps_th[par][:], AF.Sigmoid,
                                         bias=bh_sb[:, hc:hc + 1], scale=1.0)
                    rec[("a12", t, hc)] = p.inc(
                        nc.scalar.activation(a2_ch[par][:], ps_th[par][:], AF.Copy), "s")
                    p.wait("v", "s", rec[("a12", t, hc)])
                    nc.vector.scalar_tensor_tensor(gt_ch[par][:], a2_ch[par][:],
                                                   bh05_sb[:, hc:hc + 1], a1_ch[par][:],
                                                   ALU.add, ALU.max)
                    rec[("gv", t, hc)] = p.inc(
                        nc.vector.tensor_mul(v_bf[t % 2][:, hc * TA:(hc + 1) * TA],
                                             z_ch[par][:], gt_ch[par][:]), "v")
                    sl = slice(hc * TA, (hc + 1) * TA)
                    if first:
                        init = 0.0
                    else:
                        init = carry[:, hc:hc + 1]
                        p.wait("v", "gc", rec[("carry", t - 1, hc)])
                    rec[("scan", t, hc)] = p.inc(nc.vector.tensor_tensor_scan(
                        h_bf[t % 2][:, sl], c_bf[t % 2][:, sl], v_bf[t % 2][:, sl],
                        init, ALU.mult, ALU.add), "v")
                for hc in range(DC):
                    p.wait("g", "v", rec[("scan", t, hc)])
                    rec[("carry", t, hc)] = p.inc(
                        nc.gpsimd.tensor_copy(
                            carry[:, hc:hc + 1],
                            h_bf[t % 2][:, hc * TA + TA - 1: hc * TA + TA]), "gc")
                rec[("scan_t", t)] = rec[("scan", t, DC - 1)]

            def attstage(t):
                for dc in range(DC):
                    par = dc % 2
                    p.wait("t", "v", rec[("scan_t", t)])
                    p.wait("t", "s", rec.get(("s1a", t, dc - 2),
                                             rec.get(("s1a", t - 1, dc + 6), 0)))
                    for hc in range(DC):
                        mm = nc.tensor.matmul(
                            ps_att[par][:],
                            wo_sb[:, hc * D + dc * 128: hc * D + (dc + 1) * 128],
                            h_bf[t % 2][:, hc * TA:(hc + 1) * TA],
                            start=(hc == 0), stop=(hc == DC - 1))
                    rec[("attmm", t, dc)] = p.inc(mm, "t")
                    sl = slice(dc * TA, (dc + 1) * TA)
                    p.wait("s", "t", rec[("attmm", t, dc)])
                    p.wait("s", "v", rec.get(("s1", t - 1, dc), 0))
                    rec[("s1a", t, dc)] = p.inc(
                        nc.scalar.activation(s1tmp[:, sl], ps_att[par][:], AF.Identity,
                                             bias=bo_sb[:, dc:dc + 1], scale=1.0), "s")
                    p.wait("v", "s", rec[("s1a", t, dc)])
                    p.wait("v", "t", rec.get(("stats", t - 2), 0))
                    rec[("s1", t, dc)] = p.inc(
                        nc.vector.tensor_tensor(
                            s1_bf[t % 2][:, sl], s1tmp[:, sl],
                            x_bf[t % 3][:, sl], ALU.add), "v")
                for dc in range(DC):
                    sl = slice(dc * TA, (dc + 1) * TA)
                    p.wait("s", "v", rec[("s1", t, dc)])
                    p.wait("s", "t", rec.get(("stats", t - 1), 0))
                    rec[("sq", t, dc)] = p.inc(
                        nc.scalar.activation(sq_bf[:, sl], s1_bf[t % 2][:, sl],
                                             AF.Square), "s")

            def statsA(t):
                p.wait("t", "v", rec[("s1", t, DC - 1)])
                p.wait("t", "s", rec[("sq", t, DC - 1)])
                p.wait("t", "v", rec.get(("x1", t - 1, DC - 1), 0))
                for dc in range(DC):
                    sl = slice(dc * TA, (dc + 1) * TA)
                    nc.tensor.matmul(ps_st1[0:1, :], ones_col_bf[:, 0:1],
                                     s1_bf[t % 2][:, sl],
                                     start=(dc == 0), stop=(dc == DC - 1))
                for dc in range(DC):
                    sl = slice(dc * TA, (dc + 1) * TA)
                    mm = nc.tensor.matmul(ps_st2[0:1, :], ones_col_bf[:, 0:1],
                                          sq_bf[:, sl],
                                          start=(dc == 0), stop=(dc == DC - 1))
                rec[("stats", t)] = p.inc(mm, "t")

            def rowsA(t):
                p.wait("v", "t", rec[("stats", t)])
                nc.vector.tensor_scalar_mul(rows_m[:], ps_st1[0:1, :], -1.0 / D)
                nc.vector.tensor_mul(rows_t[:], rows_m[:], rows_m[:])
                rec[("veps", t)] = p.inc(
                    nc.vector.scalar_tensor_tensor(rows_t[:], ps_st2[0:1, :],
                                                   1.0 / D, rows_t[:],
                                                   ALU.mult, ALU.subtract), "v")
                p.wait("s", "v", rec[("veps", t)])
                rec[("sd", t)] = p.inc(
                    nc.scalar.activation(rows_sd[:], rows_t[:], AF.Sqrt,
                                         bias=eps_sb[0:1, 0:1], scale=1.0), "s")
                p.wait("v", "s", rec[("sd", t)])
                nc.vector.reciprocal(rows_a[:], rows_sd[:])
                rec[("rows", t)] = p.inc(
                    nc.vector.tensor_mul(rows_b[:], rows_m[:], rows_a[:]), "v")

            def bcastA(t):
                p.wait("t", "v", rec[("rows", t)])
                nc.tensor.matmul(ps_st1[:, :], ones_row[0:1, :], rows_a[:],
                                 start=True, stop=True)
                rec[("bcast", t)] = p.inc(
                    nc.tensor.matmul(ps_st2[:, :], ones_row[0:1, :], rows_b[:],
                                     start=True, stop=True), "t")

            def x1apply(t):
                for dc in range(DC):
                    sl = slice(dc * TA, (dc + 1) * TA)
                    par = dc % 2
                    p.wait("v", "t", rec[("bcast", t)])
                    p.wait("v", f"gx1{t % 2}", rec.get(("x1st_t", t - 2), 0))
                    nc.vector.tensor_tensor(xtmp_ch[par][:], s1_bf[t % 2][:, sl],
                                            ps_st1[:, :], ALU.mult)
                    rec[("x1", t, dc)] = p.inc(
                        nc.vector.tensor_tensor(x1t[t % 2][:, sl], xtmp_ch[par][:],
                                                ps_st2[:, :], ALU.add), "v")

            def x1store(t):
                b, soff = t // TPB, (t % TPB) * TA
                for dc in range(DC):
                    sl = slice(dc * TA, (dc + 1) * TA)
                    p.wait("g", "v", rec[("x1", t, dc)])
                    rec[("x1st", t, dc)] = p.inc(
                        nc.gpsimd.dma_start(x1s[b, dc, :, soff:soff + TA],
                                            x1t[t % 2][:, sl]), f"gx1{t % 2}", 16)
                rec[("x1st_t", t)] = rec[("x1st", t, DC - 1)]

            loadA(0)
            for s in range(NTA + 2):
                if s < NTA:
                    kth(s)
                if s - 2 >= 0:
                    statsA(s - 2)
                    rowsA(s - 2)
                if s + 1 < NTA:
                    loadA(s + 1)
                if 0 <= s - 1 < NTA:
                    attstage(s - 1)
                if s - 2 >= 0:
                    bcastA(s - 2)
                    x1apply(s - 2)
                    x1store(s - 2)

            finA = p.allcnt()

        def barrier(fin):
            for cons in ("q", "g", "t", "v", "s"):
                for prod, val in fin.items():
                    ceng = {"q": "sync", "g": "gpsimd"}.get(cons)
                    p.wait(cons, prod, val)

        stF = ExitStack()
        with stF:
            sbF = lambda n, shape, d=F32: stF.enter_context(nc.sbuf_tensor(n, shape, d))
            wexp_sb = sbF("wexp_sb", [128, 4 * DC * D], BF16)
            wstF = sbF("wstF", [128, 4 * D], F32)
            rw_f32 = sbF("rw_f32", [128, 64], F32)
            rw_bf = sbF("rw_bf", [128, 64], BF16)
            rb_f32 = sbF("rb_f32", [1, 8], F32)
            rb_bf = sbF("rb_bf", [1, 8], BF16)
            eb_f32 = sbF("eb_f32", [4, D], F32)
            eb_bf = sbF("eb_bf", [4, D], BF16)
            esel_bf = sbF("esel_bf", [8, 8 * 128], BF16)
            x1_bf = [sbF(f"fx1_{i}", [128, DC * TF], BF16) for i in range(2)]
            xe_sb = [sbF(f"xe_{e}", [128, DC * TF], BF16) for e in range(4)]
            bsel = [sbF(f"bsel{i}", [128, 4 * TF], BF16) for i in range(2)]
            yp_t = [sbF(f"yp_t{i}", [128, DC * TF], BF16) for i in range(2)]

            ps_y = [stF.enter_context(nc.psum_tensor(f"ps_y{i}", [128, TF], F32))
                    for i in range(4)]
            ps_r1 = stF.enter_context(nc.psum_tensor("ps_r1", [128, TF], F32))
            ps_r2 = stF.enter_context(nc.psum_tensor("ps_r2", [128, TF], F32))
            ps_r3 = stF.enter_context(nc.psum_tensor("ps_r3", [128, TF], F32))

            barrier(finA)

            def load_experts(e_base, prewait=0, els=(0, 1, 2, 3)):
                if prewait:
                    p.wait("q", "t", prewait)
                    p.wait("g", "t", prewait)
                    p.wait("v", "t", prewait)
                for el in els:
                    for kc in range(DC):
                        g = el * DC + kc
                        par = g % 4
                        qpar = g % 2
                        qeng = nc.sync if qpar == 0 else nc.gpsimd
                        p.wait("q" if qpar == 0 else "g", "v",
                               rec.get(("ecast", e_base, g - 4), 0))
                        qv = p.inc(qeng.dma_start(
                            wstF[:, par * D:(par + 1) * D],
                            wexp[e_base + el, kc * 128:(kc + 1) * 128, :]), f"qe{par}", 16)
                        p.wait("v", f"qe{par}", qv)
                        rec[("ecast", e_base, g)] = p.inc(nc.vector.tensor_copy(
                            wexp_sb[:, (el * DC + kc) * D:(el * DC + kc + 1) * D],
                            wstF[:, par * D:(par + 1) * D]), "v")
                    rec[("ewx", e_base, el)] = rec[("ecast", e_base, el * DC + DC - 1)]
                    if el == 3:
                        rec[("ew", e_base)] = rec[("ecast", e_base, 4 * DC - 1)]

            def load_consts_f(pass_ix):
                if pass_ix == 0:
                    for kc in range(DC):
                        p.inc(nc.sync.dma_start(rw_f32[:, kc * 8:(kc + 1) * 8],
                                                rw_d[kc * 128:(kc + 1) * 128, :]), "qc", 16)
                    p.inc(nc.sync.dma_start(rb_f32[:], rb_d), "qc", 16)
                    p.inc(nc.sync.dma_start(esel_bf[:], esel8_d), "qc", 16)
                    qv = p.inc(nc.sync.dma_start(eb_f32[:], eb_d[0:4, :]), "qc", 16)
                    p.wait("v", "qc", qv)
                    nc.vector.tensor_copy(rw_bf[:], rw_f32[:])
                    nc.vector.tensor_copy(rb_bf[:], rb_f32[:])
                else:
                    p.inc(nc.sync.dma_start(esel_bf[0:4, 0:4 * 128], esel4_d), "qc", 16)
                    qv = p.inc(nc.sync.dma_start(eb_f32[:], eb_d[4:8, :]), "qc", 16)
                    p.wait("v", "qc", qv)
                rec[("fconsts", pass_ix)] = p.inc(
                    nc.vector.tensor_copy(eb_bf[:], eb_f32[:]), "v")

            def load_x1(t, pass_ix):
                b, soff = t // TPBF, (t % TPBF) * TF
                par = t % 2
                for dc in range(DC):
                    p.wait("q", "v", rec.get(("xe", pass_ix, t - 2, 3, DC - 1), 0))
                    if pass_ix == 1:
                        p.wait("q", "v", rec.get(("s2", t - 2, DC - 1), 0))
                    p.inc(nc.sync.dma_start(
                        x1_bf[par][:, dc * TF:(dc + 1) * TF],
                        x1s[b, dc, :, soff:soff + TF]), f"qf{par}", 16)
                rec[("x1ld", t)] = p.cnt[f"qf{par}"]

            def load_yp(t):
                b, soff = t // TPBF, (t % TPBF) * TF
                par = t % 2
                for dc in range(DC):
                    p.wait("q", "v", rec.get(("s2", t - 2, DC - 1), 0))
                    p.inc(nc.sync.dma_start(
                        yp_t[par][:, dc * TF:(dc + 1) * TF],
                        yps[b, dc, :, soff:soff + TF]), f"qy{par}", 16)
                rec[("ypld", t)] = p.cnt[f"qy{par}"]

            ps_r4 = stF.enter_context(nc.psum_tensor("ps_r4", [128, TF], F32))

            stF0 = ExitStack()
            with stF0:
                sb0 = lambda n, shape, d=F32: stF0.enter_context(
                    nc.sbuf_tensor(n, shape, d))
                l_fm = [sb0(f"l_fm{i}", [8, TF], F32) for i in range(2)]
                e_fm = [sb0(f"e_fm{i}", [8, TF], BF16) for i in range(2)]
                oh_fm = [sb0(f"oh_fm{i}", [8, TF], F32) for i in range(2)]
                sel_fm = [sb0(f"sel_fm{i}", [8, TF], BF16) for i in range(2)]
                sh1 = sb0("sh1", [8, TF], F32)
                m4 = sb0("m4", [4, TF], F32)
                t2_ = sb0("t2_", [4, TF], F32)
                m2 = sb0("m2", [2, TF], F32)
                t3_ = sb0("t3_", [2, TF], F32)
                m_row = [sb0(f"m_row{i}", [1, TF], F32) for i in range(2)]
                num_row = [sb0(f"num_row{i}", [1, TF], F32) for i in range(2)]
                gate_row = [sb0(f"gate_row{i}", [1, TF], F32) for i in range(2)]
                rden_row = sb0("rden_row", [1, TF], F32)

                MASK4 = [(q + 4) % 32 for q in range(32)]
                MASK2 = [(q + 2) % 32 for q in range(32)]
                MASK1 = [(q + 1) % 32 for q in range(32)]

                def routerA(t):
                    par = t % 2
                    p.wait("t", f"qf{t % 2}", rec[("x1ld", t)])
                    p.wait("t", "v", rec[("fconsts", 0)])
                    p.wait("t", "s", rec.get(("lfm", t - 1), 0))
                    for kc in range(DC):
                        nc.tensor.matmul(ps_r1[0:8, :], rw_bf[:, kc * 8:(kc + 1) * 8],
                                         x1_bf[t % 2][:, kc * TF:(kc + 1) * TF],
                                         start=(kc == 0), stop=False)
                    rec[("lmm", t)] = p.inc(
                        nc.tensor.matmul(ps_r1[0:8, :], rb_bf[0:1, :],
                                         ones_tf_bf[0:1, :], start=False, stop=True), "t")
                    p.wait("s", "t", rec[("lmm", t)])
                    nc.scalar.activation(l_fm[par][:], ps_r1[0:8, :], AF.Copy)
                    rec[("lfm", t)] = p.inc(
                        nc.scalar.activation(e_fm[par][:], ps_r1[0:8, :], AF.Exp), "s")
                    p.wait("v", "s", rec[("lfm", t)])
                    nc.vector.stream_shuffle(sh1[:], l_fm[par][:], MASK4)
                    nc.vector.tensor_max(m4[:], sh1[0:4, :], l_fm[par][0:4, :])
                    nc.vector.stream_shuffle(t2_[:], m4[:], MASK2)
                    nc.vector.tensor_max(m2[:], m4[0:2, :], t2_[0:2, :])
                    nc.vector.stream_shuffle(t3_[:], m2[:], MASK1)
                    rec[("mrow", t)] = p.inc(
                        nc.vector.tensor_max(m_row[par][:], m2[0:1, :], t3_[0:1, :]), "v")

                def routerB(t):
                    par = t % 2
                    p.wait("t", "v", rec[("mrow", t)])
                    rec[("mbc", t)] = p.inc(
                        nc.tensor.matmul(ps_r2[0:8, :], ones_row[0:1, 0:8],
                                         m_row[par][:], start=True, stop=True), "t")
                    rec[("den", t)] = p.inc(
                        nc.tensor.matmul(ps_r3[0:1, :], ones8_bf[:, 0:1], e_fm[par][:],
                                         start=True, stop=True), "t")
                    p.wait("v", "t", rec[("mbc", t)])
                    rec[("oh", t)] = p.inc(
                        nc.vector.tensor_tensor(oh_fm[par][:], l_fm[par][:],
                                                ps_r2[0:8, :], ALU.is_ge), "v")
                    p.wait("s", "v", rec[("mrow", t)])
                    rec[("num", t)] = p.inc(
                        nc.scalar.activation(num_row[par][:], m_row[par][:], AF.Exp), "s")
                    p.wait("v", "t", rec[("den", t)])
                    p.wait("v", "s", rec[("num", t)])
                    nc.vector.reciprocal(rden_row[:], ps_r3[0:1, :])
                    rec[("gate", t)] = p.inc(
                        nc.vector.tensor_mul(gate_row[par][:], num_row[par][:],
                                             rden_row[:]), "v")

                def routerC(t):
                    par = t % 2
                    p.wait("t", "v", rec[("oh", t)])
                    p.wait("t", "v", rec[("gate", t)])
                    rec[("gbc", t)] = p.inc(
                        nc.tensor.matmul(ps_r2[0:8, :], ones_row[0:1, 0:8],
                                         gate_row[par][:], start=True, stop=True), "t")
                    p.wait("v", "t", rec[("gbc", t)])
                    rec[("sel", t)] = p.inc(
                        nc.vector.tensor_tensor(sel_fm[par][:], oh_fm[par][:],
                                                ps_r2[0:8, :], ALU.mult), "v")
                    p.wait("g", "v", rec[("sel", t)])
                    rec[("selst", t)] = p.inc(
                        nc.gpsimd.dma_start(sels[:, t * TF:(t + 1) * TF], sel_fm[par][:]),
                        "gsel", 16)

                def routerD(t):
                    par = t % 2
                    for e in range(4):
                        bank = ps_r4 if e % 2 == 0 else ps_r3
                        p.wait("t", "v", rec[("sel", t)])
                        if e >= 2:
                            p.wait("t", "s", rec.get(("bselev", t, e - 2), 0))
                        rec[("bselmm", t, e)] = p.inc(
                            nc.tensor.matmul(bank[:, :],
                                             esel_bf[:, e * 128:(e + 1) * 128],
                                             sel_fm[par][:], start=True, stop=True), "t")
                        p.wait("s", "t", rec[("bselmm", t, e)])
                        p.wait("s", "v", rec.get(("xe", 0, t - 2, 3, DC - 1), 0))
                        rec[("bselev", t, e)] = p.inc(
                            nc.scalar.activation(bsel[par][:, e * TF:(e + 1) * TF],
                                                 bank[:, :], AF.Copy), "s")

                def xestage(t, pass_ix):
                    par = t % 2
                    for e in range(4):
                        for kc in range(DC):
                            p.wait("v", f"qf{t % 2}", rec[("x1ld", t)])
                            if pass_ix == 0:
                                p.wait("v", "s", rec[("bselev", t, e)])
                            else:
                                p.wait("v", "s", rec[("bselev1", t, e)])
                            p.wait("v", "t",
                                   rec.get(("emm", pass_ix, t - 1, 1, e, DC - 1), 0))
                            rec[("xe", pass_ix, t, e, kc)] = p.inc(
                                nc.vector.tensor_mul(
                                    xe_sb[e][:, kc * TF:(kc + 1) * TF],
                                    x1_bf[par][:, kc * TF:(kc + 1) * TF],
                                    bsel[par][:, e * TF:(e + 1) * TF]), "v")

                def expertsHalf(t, pass_ix, e_base, half):
                    par = t % 2
                    for e in range(4):
                        p.wait("t", "v", rec[("ewx", e_base, e)])
                        for kc in range(DC):
                            p.wait("t", "v", rec[("xe", pass_ix, t, e, kc)])
                            if half == 1 and e == 0 and kc == 0:
                                p.wait("t", "v", rec.get(("yev", pass_ix, t, 0, 3), 0))
                            for dcl in range(4):
                                dc = half * 4 + dcl
                                mm = nc.tensor.matmul(
                                    ps_y[dcl][:],
                                    wexp_sb[:, (e * DC + kc) * D + dc * 128:
                                            (e * DC + kc) * D + (dc + 1) * 128],
                                    xe_sb[e][:, kc * TF:(kc + 1) * TF],
                                    start=(e == 0 and kc == 0), stop=False)
                            rec[("emm", pass_ix, t, half, e, kc)] = p.inc(mm, "t")
                    p.wait("t", "v", rec[("fconsts", pass_ix)])
                    if pass_ix == 0:
                        p.wait("t", "v", rec[("sel", t)])
                        selsrc = sel_fm[par][0:4, :]
                    else:
                        p.wait("t", f"qs4{t % 2}", rec[("sel4ld", t)])
                        selsrc = sel4_t[par][0:4, :]
                    for dcl in range(4):
                        dc = half * 4 + dcl
                        mm = nc.tensor.matmul(
                            ps_y[dcl][:], eb_bf[0:4, dc * 128:(dc + 1) * 128],
                            selsrc, start=False, stop=True)
                        rec[("bias", pass_ix, t, half, dcl)] = p.inc(mm, "t")
                    for dcl in range(4):
                        dc = half * 4 + dcl
                        sl = slice(dc * TF, (dc + 1) * TF)
                        p.wait("v", "t", rec[("bias", pass_ix, t, half, dcl)])
                        if pass_ix == 0:
                            p.wait("v", f"gyp{t % 2}", rec.get(("ypst_t", t - 2), 0))
                            rec[("yev", pass_ix, t, half, dcl)] = p.inc(
                                nc.vector.tensor_copy(yp_t[par][:, sl],
                                                      ps_y[dcl][:]), "v")
                        else:
                            p.wait("v", f"qy{t % 2}", rec[("ypld", t)])
                            p.wait("v", "t", rec.get(("stats2", t - 2), 0))
                            nc.vector.tensor_tensor(tmpf_ch[dc % 2][:], ps_y[dcl][:],
                                                    yp_t[par][:, sl], ALU.add)
                            rec[("s2", t, dc)] = p.inc(
                                nc.vector.tensor_tensor(
                                    s2_bf[par][:, sl], tmpf_ch[dc % 2][:],
                                    x1_bf[par][:, sl], ALU.add), "v")
                            rec[("yev", pass_ix, t, half, dcl)] = rec[("s2", t, dc)]

                def sqstage(t, half):
                    for dcl in range(4):
                        dc = half * 4 + dcl
                        sl = slice(dc * TF, (dc + 1) * TF)
                        p.wait("s", "v", rec[("s2", t, dc)])
                        p.wait("s", "t", rec.get(("stats2", t - 1), 0))
                        rec[("sq2", t, dc)] = p.inc(
                            nc.scalar.activation(sq2_bf[:, sl], s2_bf[t % 2][:, sl],
                                                 AF.Square), "s")

                def ypstore(t):
                    b, soff = t // TPBF, (t % TPBF) * TF
                    for dc in range(DC):
                        sl = slice(dc * TF, (dc + 1) * TF)
                        half, dcl = dc // 4, dc % 4
                        p.wait("g", "v", rec[("yev", 0, t, half, dcl)])
                        rec[("ypst", t, dc)] = p.inc(
                            nc.gpsimd.dma_start(yps[b, dc, :, soff:soff + TF],
                                                yp_t[t % 2][:, sl]), f"gyp{t % 2}", 16)
                    rec[("ypst_t", t)] = rec[("ypst", t, DC - 1)]

                load_consts_f(0)
                load_x1(0, 0)
                load_x1(1, 0)
                load_experts(0, els=(0,))
                routerA(0)
                routerB(0)
                routerC(0)
                routerD(0)
                load_experts(0, els=(1, 2, 3))
                xestage(0, 0)
                for s in range(NTF):
                    if s + 1 < NTF:
                        routerA(s + 1)
                    expertsHalf(s, 0, 0, 0)
                    if s + 1 < NTF:
                        routerB(s + 1)
                    expertsHalf(s, 0, 0, 1)
                    if s + 1 < NTF:
                        routerC(s + 1)
                        routerD(s + 1)
                    ypstore(s)
                    if s + 2 < NTF:
                        load_x1(s + 2, 0)
                    if s + 1 < NTF:
                        xestage(s + 1, 0)

                fin0 = p.allcnt()

            stF1 = ExitStack()
            with stF1:
                sb1 = lambda n, shape, d=F32: stF1.enter_context(
                    nc.sbuf_tensor(n, shape, d))
                sel4_t = [sb1(f"sel4_{i}", [4, TF], BF16) for i in range(2)]
                s2_bf = [sb1(f"s2_bf{i}", [128, DC * TF], BF16) for i in range(2)]
                sq2_bf = sb1("sq2_bf", [128, DC * TF], BF16)
                tmpf_ch = [sb1(f"tmpf_{i}", [128, TF], F32) for i in range(2)]
                x2_ch = [sb1(f"x2_{i}", [128, TF], F32) for i in range(2)]
                rows2_m = sb1("r2m", [1, TF], F32)
                rows2_a = sb1("r2a", [1, TF], F32)
                rows2_b = sb1("r2b", [1, TF], F32)
                rows2_t = sb1("r2t", [1, TF], F32)
                rows2_sd = sb1("r2sd", [1, TF], F32)

                barrier(fin0)
                load_consts_f(1)
                prew1 = rec[("emm", 0, NTF - 1, 1, 3, DC - 1)]

                def load_sel4(t):
                    par = t % 2
                    p.wait("q", "t", rec.get(("bias", 1, t - 2, 1, 3), 0))
                    p.inc(nc.sync.dma_start(sel4_t[par][:],
                                            sels[4:8, t * TF:(t + 1) * TF]), f"qs4{par}", 16)
                    rec[("sel4ld", t)] = p.cnt[f"qs4{par}"]

                def bsel1(t):
                    par = t % 2
                    for e in range(4):
                        bank = ps_r4 if e % 2 == 0 else ps_r3
                        p.wait("t", f"qs4{t % 2}", rec[("sel4ld", t)])
                        p.wait("t", "v", rec[("fconsts", 1)])
                        p.wait("t", "s", rec.get(("bselev1", t - 1, 3), 0))
                        if e >= 2:
                            p.wait("t", "s", rec.get(("bselev1", t, e - 2), 0))
                        rec[("bselmm1", t, e)] = p.inc(
                            nc.tensor.matmul(bank[:, :],
                                             esel_bf[0:4, e * 128:(e + 1) * 128],
                                             sel4_t[par][:], start=True, stop=True), "t")
                        p.wait("s", "t", rec[("bselmm1", t, e)])
                        p.wait("s", "v", rec.get(("xe", 1, t - 2, 3, DC - 1), 0))
                        rec[("bselev1", t, e)] = p.inc(
                            nc.scalar.activation(bsel[par][:, e * TF:(e + 1) * TF],
                                                 bank[:, :], AF.Copy), "s")

                def stats2(t):
                    p.wait("t", "v", rec[("s2", t, DC - 1)])
                    p.wait("t", "s", rec[("sq2", t, DC - 1)])
                    p.wait("t", "v", rec.get(("x2", t - 1, DC - 1), 0))
                    for dc in range(DC):
                        sl = slice(dc * TF, (dc + 1) * TF)
                        nc.tensor.matmul(ps_r1[0:1, :], ones_col_bf[:, 0:1],
                                         s2_bf[t % 2][:, sl], start=(dc == 0),
                                         stop=(dc == DC - 1))
                    for dc in range(DC):
                        sl = slice(dc * TF, (dc + 1) * TF)
                        mm = nc.tensor.matmul(ps_r2[0:1, :], ones_col_bf[:, 0:1],
                                              sq2_bf[:, sl], start=(dc == 0),
                                              stop=(dc == DC - 1))
                    rec[("stats2", t)] = p.inc(mm, "t")

                def rows2(t):
                    p.wait("v", "t", rec[("stats2", t)])
                    nc.vector.tensor_scalar_mul(rows2_m[:], ps_r1[0:1, :], -1.0 / D)
                    nc.vector.tensor_mul(rows2_t[:], rows2_m[:], rows2_m[:])
                    rec[("veps2", t)] = p.inc(
                        nc.vector.scalar_tensor_tensor(rows2_t[:], ps_r2[0:1, :],
                                                       1.0 / D, rows2_t[:],
                                                       ALU.mult, ALU.subtract), "v")
                    p.wait("s", "v", rec[("veps2", t)])
                    rec[("sd2", t)] = p.inc(
                        nc.scalar.activation(rows2_sd[:], rows2_t[:], AF.Sqrt,
                                             bias=eps_sb[0:1, 0:1], scale=1.0), "s")
                    p.wait("v", "s", rec[("sd2", t)])
                    nc.vector.reciprocal(rows2_a[:], rows2_sd[:])
                    rec[("rows2", t)] = p.inc(
                        nc.vector.tensor_mul(rows2_b[:], rows2_m[:], rows2_a[:]), "v")

                def bcast2(t):
                    p.wait("t", "v", rec[("rows2", t)])
                    nc.tensor.matmul(ps_r1[:, :], ones_row[0:1, :], rows2_a[:],
                                     start=True, stop=True)
                    rec[("bcast2", t)] = p.inc(
                        nc.tensor.matmul(ps_r2[:, :], ones_row[0:1, :], rows2_b[:],
                                         start=True, stop=True), "t")

                def x2apply(t):
                    b, soff = t // TPBF, (t % TPBF) * TF
                    for dc in range(DC):
                        sl = slice(dc * TF, (dc + 1) * TF)
                        par = dc % 2
                        p.wait("v", "t", rec[("bcast2", t)])
                        p.wait("v", f"gout{dc % 2}", rec.get(("outst", t, dc - 2),
                                                 rec.get(("outst", t - 1, dc + 6), 0)))
                        nc.vector.tensor_tensor(tmpf_ch[par][:], s2_bf[t % 2][:, sl],
                                                ps_r1[:, :], ALU.mult)
                        rec[("x2", t, dc)] = p.inc(
                            nc.vector.tensor_tensor(x2_ch[par][:], tmpf_ch[par][:],
                                                    ps_r2[:, :], ALU.add), "v")
                        p.wait("g", "v", rec[("x2", t, dc)])
                        rec[("outst", t, dc)] = p.inc(
                            nc.gpsimd.dma_start(
                                out_fm[b, dc * 128:(dc + 1) * 128, soff:soff + TF],
                                x2_ch[par][:]), f"gout{dc % 2}", 16)

                load_x1(0, 1)
                load_yp(0)
                load_sel4(0)
                bsel1(0)
                xestage(0, 1)
                load_x1(1, 1)
                load_yp(1)
                load_sel4(1)
                load_experts(4, prewait=prew1)
                for s in range(NTF):
                    expertsHalf(s, 1, 4, 0)
                    if s - 1 >= 0:
                        stats2(s - 1)
                        rows2(s - 1)
                    if s + 1 < NTF:
                        bsel1(s + 1)
                    sqstage(s, 0)
                    expertsHalf(s, 1, 4, 1)
                    if s - 1 >= 0:
                        bcast2(s - 1)
                    sqstage(s, 1)
                    if s + 1 < NTF:
                        xestage(s + 1, 1)
                    if s - 1 >= 0:
                        x2apply(s - 1)
                    if s + 2 < NTF:
                        load_x1(s + 2, 1)
                        load_yp(s + 2)
                        load_sel4(s + 2)
                stats2(NTF - 1)
                rows2(NTF - 1)
                bcast2(NTF - 1)
                x2apply(NTF - 1)

                nc.gpsimd.wait_ge(p.sems["gout0"], p.cnt["gout0"])
                nc.gpsimd.wait_ge(p.sems["gout1"], p.cnt["gout1"])
    return nc


_NC_CACHE = {}


def kernel(L0_attn_x, L0_attn_Wz, L0_attn_bz, L0_attn_Wh, L0_attn_bh, L0_attn_Wo,
           L0_attn_bo, L0_ffn_router_w, L0_ffn_router_b, L0_ffn_expert_w,
           L0_ffn_expert_b):
    x = np.asarray(L0_attn_x, dtype=np.float32)
    wz = np.asarray(L0_attn_Wz, dtype=np.float32)
    wh = np.asarray(L0_attn_Wh, dtype=np.float32)
    wo = np.asarray(L0_attn_Wo, dtype=np.float32)
    bz = np.ascontiguousarray(np.asarray(L0_attn_bz, dtype=np.float32))
    bh = np.ascontiguousarray(np.asarray(L0_attn_bh, dtype=np.float32))
    bo = np.ascontiguousarray(np.asarray(L0_attn_bo, dtype=np.float32))
    rw = np.ascontiguousarray(np.asarray(L0_ffn_router_w, dtype=np.float32))
    rb = np.ascontiguousarray(
        np.asarray(L0_ffn_router_b, dtype=np.float32)).reshape(1, E)
    ew = np.ascontiguousarray(np.asarray(L0_ffn_expert_w, dtype=np.float32))
    eb = np.ascontiguousarray(np.asarray(L0_ffn_expert_b, dtype=np.float32))

    wzt = np.ascontiguousarray(wz.T)
    wht = np.ascontiguousarray(wh.T)
    wot = np.ascontiguousarray(wo.T)

    esel8 = np.zeros((E, E * 128), bfdt)
    for e in range(E):
        esel8[e, e * 128:(e + 1) * 128] = bfdt(1.0)
    esel4 = np.zeros((4, 4 * 128), bfdt)
    for e in range(4):
        esel4[e, e * 128:(e + 1) * 128] = bfdt(1.0)

    if "nc" not in _NC_CACHE:
        _NC_CACHE["nc"] = build_program()
    nc = _NC_CACHE["nc"]

    in_maps = []
    for i in range(NCORE):
        xs = np.ascontiguousarray(x[i * BPC:(i + 1) * BPC].transpose(0, 2, 1))
        in_maps.append({
            "x_fm": xs, "wzt": wzt, "wht": wht, "wot": wot, "wexp": ew,
            "bz_d": bz, "bh_d": bh, "bo_d": bo, "rw_d": rw, "rb_d": rb,
            "eb_d": eb, "esel8_d": esel8, "esel4_d": esel4,
        })
    res = run_bass_kernel_spmd(nc, in_maps, core_ids=list(range(NCORE)))
    outs = [np.asarray(r["out_fm"]).transpose(0, 2, 1) for r in res.results]
    return np.ascontiguousarray(np.concatenate(outs, axis=0)).astype(np.float32)
